# version 09 (final)
# baseline (speedup 1.0000x reference)
import math
import numpy as np
import ml_dtypes
import concourse.bass as bass
import concourse.mybir as mybir
from concourse.bass_utils import run_bass_kernel_spmd
from contextlib import ExitStack

F32 = mybir.dt.float32
BF16 = mybir.dt.bfloat16
AX = mybir.AxisListType
ALU = mybir.AluOpType
ACT = mybir.ActivationFunctionType
BF = ml_dtypes.bfloat16

B, T, C, E = 4, 4096, 1024, 8
NCN = 8
JS = T // NCN
R = B * JS
NT = T // 128
NFC = C // 128
EPS = 1e-5
FULLSYNC = True


def _alibi_slopes(n):
    start = 2.0 ** (-(2.0 ** (-(math.log2(n) - 3))))
    return np.array([start * start**i for i in range(n)], dtype=np.float32)


def _hi_lo(a):
    hi = a.astype(BF).astype(np.float32)
    lo = (a - hi).astype(BF).astype(np.float32)
    return hi, lo


def build_nc():
    nc = bass.Bass("TRN2", target_bir_lowering=False, num_devices=NCN)

    d_xb = nc.dram_tensor("xb", [T, C], F32, kind="ExternalInput").ap()
    d_xsl = nc.dram_tensor("xsl", [B, JS, C], F32, kind="ExternalInput").ap()
    d_wqkv = nc.dram_tensor("wqkv", [4, C, 384], F32, kind="ExternalInput").ap()
    d_kaug = nc.dram_tensor("kaug", [8, 6, T], F32, kind="ExternalInput").ap()
    d_qaug = nc.dram_tensor("qaug", [6, T], F32, kind="ExternalInput").ap()
    d_wproj = nc.dram_tensor("wproj", [NFC, C, 128], F32, kind="ExternalInput").ap()
    d_win = nc.dram_tensor("win", [24, C, 128], F32, kind="ExternalInput").ap()
    d_inb = nc.dram_tensor("inb", [24, 128], F32, kind="ExternalInput").ap()
    d_wout = nc.dram_tensor("wout", [NFC, C, 128], F32, kind="ExternalInput").ap()
    d_outb = nc.dram_tensor("outb", [NFC, 128], F32, kind="ExternalInput").ap()
    d_g1 = nc.dram_tensor("g1", [NFC, 128], F32, kind="ExternalInput").ap()
    d_bb1 = nc.dram_tensor("bb1", [NFC, 128], F32, kind="ExternalInput").ap()
    d_rw = nc.dram_tensor("rw", [C, 8], F32, kind="ExternalInput").ap()
    d_rb = nc.dram_tensor("rb", [1, 8], F32, kind="ExternalInput").ap()
    d_w1 = nc.dram_tensor("w1", [E, NFC, C, 128], F32, kind="ExternalInput").ap()
    d_b1 = nc.dram_tensor("b1", [E, C], F32, kind="ExternalInput").ap()
    d_w2 = nc.dram_tensor("w2", [E, NFC, C, 128], F32, kind="ExternalInput").ap()
    d_b2 = nc.dram_tensor("b2", [E, C], F32, kind="ExternalInput").ap()
    d_g2 = nc.dram_tensor("g2", [NFC, 128], F32, kind="ExternalInput").ap()
    d_bb2 = nc.dram_tensor("bb2", [NFC, 128], F32, kind="ExternalInput").ap()
    d_out = nc.dram_tensor("out", [B, JS, C], F32, kind="ExternalOutput").ap()
    d_ysh = nc.dram_tensor("ysh", [32, 128, T], BF16, addr_space="Shared",
                           kind="Internal").ap()
    d_agin = nc.dram_tensor("agin", [4, 128, T], BF16, kind="Internal").ap()

    ident_d = nc.inline_tensor(np.eye(128, dtype=BF), "identb").ap()
    cm = np.zeros((128, 128), dtype=np.float32)
    ii0 = np.arange(128)
    cm[ii0[:, None] < ii0[None, :]] = -1e9
    causm_d = nc.inline_tensor(cm.astype(BF), "causm").ap()
    bm = np.full((128, 128), -1e9, dtype=np.float32)
    for g in range(32):
        bm[4 * g:4 * g + 4, 4 * g:4 * g + 4] = 0.0
    bdm_d = nc.inline_tensor(bm.astype(BF), "bdm").ap()
    onesc_d = nc.inline_tensor(np.full((128, 1), 1.0 / 1024.0, dtype=BF),
                               "onesc").ap()
    ones1_d = nc.inline_tensor(np.ones((1, 128), dtype=BF), "ones1").ap()
    ones1f_d = nc.inline_tensor(np.ones((1, 128), dtype=np.float32), "ones1f").ap()
    iota8_d = nc.inline_tensor(
        np.broadcast_to(np.arange(8, dtype=np.float32), (128, 8)).copy(),
        "iota8").ap()

    pid = nc.partition_id()

    root = ExitStack()
    ctx = root.enter_context

    sems, cnt, last_wait = {}, {}, {}
    for name in ["T", "V", "S", "X", "G", "O"]:
        sems[name] = ctx(nc.semaphore("sem" + name))
        cnt[name] = 0

    def inc(instr, name, step=None):
        if step is None:
            step = 16 if name in ("X", "G", "O") else 1
        instr.then_inc(sems[name], step)
        cnt[name] += step
        return cnt[name]

    def wt(eng, ename, name, val=None):
        if val is None:
            val = cnt[name]
        key = (ename, name)
        if last_wait.get(key, 0) >= val or val <= 0:
            return
        eng.wait_ge(sems[name], val)
        last_wait[key] = val

    block = ctx(nc.Block())

    def sec(name, f):
        if FULLSYNC:
            def g(eng):
                for s in ["T", "V", "S", "X", "G", "O"]:
                    wt(eng, name, s, cnt[s])
                f(eng)
            getattr(block, name)(g)
        else:
            getattr(block, name)(f)

    stacks, sb = {}, {}

    def salloc(name, p, w, dt):
        st = ExitStack()
        sb[name] = st.enter_context(nc.sbuf_tensor(name, [p, w], dt))
        stacks[name] = st
        return sb[name]

    def sfree(name):
        stacks.pop(name).close()
        sb.pop(name)

    ps = {}
    for name in ["sc0", "sc1", "sc2", "sc3", "po", "pe"]:
        ps[name] = ctx(nc.psum_tensor(name, [128, 512], F32))
    pt0 = ctx(nc.psum_tensor("pt0", [128, 512], BF16))
    pt1 = ctx(nc.psum_tensor("pt1", [128, 512], BF16))
    ptb = [pt0, pt1]

    identS = salloc("identS", 128, 128, BF16)
    causS = salloc("causS", 128, 128, BF16)
    bdmS = salloc("bdmS", 128, 128, BF16)
    onescS = salloc("onescS", 128, 1, BF16)
    ones1S = salloc("ones1S", 1, 128, BF16)
    ones1fS = salloc("ones1fS", 1, 128, F32)
    iota8S = salloc("iota8S", 128, 8, F32)
    inbc = salloc("inbc", 128, 24, F32)
    inbcq = salloc("inbcq", 128, 8, F32)
    outbc = salloc("outbc", 128, 8, F32)
    g1c = salloc("g1c", 128, 8, F32)
    bb1c = salloc("bb1c", 128, 8, F32)
    g2c = salloc("g2c", 128, 8, F32)
    bb2c = salloc("bb2c", 128, 8, F32)
    b1c = salloc("b1c", 128, 64, F32)
    bcc = salloc("bcc", 128, 8, F32)
    b2S = salloc("b2S", 8, C, BF16)
    rbS = salloc("rbS", 1, 8, BF16)
    rwS = salloc("rwS", 128, NFC * 8, BF16)
    stagA = salloc("stagA", 24, 128, BF16)
    statrow = salloc("statrow", 1, 4 * 512, F32)
    epsc = salloc("epsc", 128, 1, F32)

    def _(v):
        ii = v.memset(epsc[:, :], float(EPS))
        inc(ii, "V")
    sec("vector", _)

    def _(sy):
        inc(sy.dma_start(identS[:, :], ident_d), "X")
        inc(sy.dma_start(causS[:, :], causm_d), "X")
        inc(sy.dma_start(bdmS[:, :], bdm_d), "X")
        inc(sy.dma_start(onescS[:, :], onesc_d), "X")
        inc(sy.dma_start(ones1S[:, :], ones1_d), "X")
        inc(sy.dma_start(ones1fS[:, :], ones1f_d), "X")
        i = sy.dma_start(iota8S[:, :], iota8_d)
        inc(i, "X")
    sec("sync", _)

    def _(g):
        inc(g.dma_start(b2S[:, :], d_b2), "G")
        inc(g.dma_start(rbS[:, :], d_rb), "G")
        i = g.dma_start(rwS.rearrange("p (c o) -> p c o", o=8),
                        d_rw.rearrange("(c p) o -> c p o", p=128)
                        .rearrange("c p o -> p c o"))
        inc(i, "G")
    sec("gpsimd", _)

    def colize(dram_ap, n, dest, dcol, scale=None):
        def _(g):
            wt(g, "gpsimd", "V")
            i = g.dma_start(stagA[0:n, :], dram_ap)
            inc(i, "G")
        sec("gpsimd", _)
        gs = cnt["G"]

        def _(t):
            wt(t, "tensor", "G", gs)
            wt(t, "tensor", "V")
            i = t.transpose(pt0[:, 0:n], stagA[0:n, :], identS[0:n, 0:n])
            inc(i, "T")
        sec("tensor", _)
        tsn = cnt["T"]

        def _(v):
            wt(v, "vector", "T", tsn)
            if scale is None:
                i = v.tensor_copy(dest[:, dcol:dcol + n], pt0[:, 0:n])
            else:
                i = v.tensor_scalar(dest[:, dcol:dcol + n], pt0[:, 0:n], scale,
                                    None, ALU.mult)
            inc(i, "V")
        sec("vector", _)

    colize(d_inb[0:24, :], 24, inbc, 0)
    colize(d_inb[0:8, :], 8, inbcq, 0, scale=0.125)
    colize(d_outb, 8, outbc, 0)
    colize(d_g1, 8, g1c, 0)
    colize(d_bb1, 8, bb1c, 0)
    colize(d_g2, 8, g2c, 0)
    colize(d_bb2, 8, bb2c, 0)
    for e in range(E):
        colize(d_b1.rearrange("e (c p) -> e c p", p=128)[e], 8, b1c, 8 * e)

    xnT = salloc("xnT", 128, NFC * T, BF16)
    QhA = salloc("QhA", 70, T, BF16)
    QhB = salloc("QhB", 70, T, BF16)
    KhA = salloc("KhA", 70, T, BF16)
    KhB = salloc("KhB", 70, T, BF16)
    VtA = salloc("VtA", 128, NT * 64, BF16)
    VtB = salloc("VtB", 128, NT * 64, BF16)
    Pst = [salloc(f"Pst{i}", 128, T, BF16) for i in range(2)]
    pTs = [salloc(f"pTs{i}", 128, T, BF16) for i in range(2)]
    O2T = [salloc(f"O2T{i}", 128, T, BF16) for i in range(2)]
    wqp = [salloc(f"wqp{i}", 128, 3072, BF16) for i in range(2)]
    xts = [salloc(f"xts{i}", 128, C, F32) for i in range(2)]
    xns = [salloc(f"xns{i}", 128, C, BF16) for i in range(2)]
    sqa = salloc("sqa", 128, C, F32)
    astat = salloc("astat", 128, 16, F32)
    nmS = salloc("nmS", 128, 24, F32)
    lsS = salloc("lsS", 128, 24, F32)
    CcS = salloc("CcS", 128, 24, F32)

    for i in range(NT):
        par = i % 2
        so = 8 * par

        def _(sy, i=i, par=par):
            if i >= 2:
                wt(sy, "sync", "S")
            ii = sy.dma_start(xts[par][:, :], d_xb[bass.ts(i, 128), :])
            inc(ii, "X")
        sec("sync", _)
        xsn = cnt["X"]

        def _(v, par=par, so=so, xsn=xsn):
            wt(v, "vector", "X", xsn)
            ii = v.reduce_sum(astat[:, so:so + 1], xts[par][:, :], axis=AX.X)
            inc(ii, "V")
        sec("vector", _)
        v1 = cnt["V"]

        def _(s, par=par, so=so, xsn=xsn):
            wt(s, "scalar", "X", xsn)
            wt(s, "scalar", "V")
            ii = s.activation(sqa[:, :], xts[par][:, :], ACT.Square,
                              accum_out=astat[:, so + 1:so + 2])
            inc(ii, "S")
        sec("scalar", _)
        s1 = cnt["S"]

        def _(v, so=so, v1=v1, s1=s1):
            wt(v, "vector", "V", v1)
            wt(v, "vector", "S", s1)
            v.tensor_scalar(astat[:, so + 2:so + 3], astat[:, so:so + 1],
                            1.0 / 1024.0, None, ALU.mult)
            ii = v.tensor_scalar(astat[:, so + 3:so + 4], astat[:, so + 1:so + 2],
                                 1.0 / 1024.0, None, ALU.mult)
            sv = inc(ii, "V")
            v.wait_ge(sems["V"], sv)
            ii = v.tensor_mul(astat[:, so + 4:so + 5], astat[:, so + 2:so + 3],
                              astat[:, so + 2:so + 3])
            sv = inc(ii, "V")
            v.wait_ge(sems["V"], sv)
            ii = v.tensor_sub(astat[:, so + 5:so + 6], astat[:, so + 3:so + 4],
                              astat[:, so + 4:so + 5])
            inc(ii, "V")
        sec("vector", _)
        v2 = cnt["V"]

        def _(s, so=so, v2=v2):
            wt(s, "scalar", "V", v2)
            ii = s.activation(astat[:, so + 4:so + 5], astat[:, so + 5:so + 6],
                              ACT.Sqrt, bias=epsc[:, 0:1])
            inc(ii, "S")
        sec("scalar", _)
        s2 = cnt["S"]

        def _(v, so=so, s2=s2):
            wt(v, "vector", "S", s2)
            ii = v.reciprocal(astat[:, so + 6:so + 7], astat[:, so + 4:so + 5])
            sv = inc(ii, "V")
            v.wait_ge(sems["V"], sv)
            ii = v.tensor_scalar(astat[:, so + 7:so + 8], astat[:, so + 6:so + 7],
                                 astat[:, so + 2:so + 3], -1.0, ALU.mult, ALU.mult)
            inc(ii, "V")
        sec("vector", _)
        v3 = cnt["V"]

        def _(s, i=i, par=par, so=so, v3=v3):
            wt(s, "scalar", "V", v3)
            if i >= 2:
                wt(s, "scalar", "T")
            ii = s.activation(xns[par][:, :], xts[par][:, :], ACT.Identity,
                              bias=astat[:, so + 7:so + 8],
                              scale=astat[:, so + 6:so + 7])
            inc(ii, "S")
        sec("scalar", _)
        s3 = cnt["S"]

        def _(t, par=par, s3=s3):
            wt(t, "tensor", "S", s3)
            wt(t, "tensor", "V")
            for g in range(2):
                for k in range(4):
                    fc = 4 * g + k
                    ii = t.transpose(ptb[g][:, 128 * k:128 * k + 128],
                                     xns[par][:, 128 * fc:128 * fc + 128],
                                     identS[:, :])
                inc(ii, "T")
        sec("tensor", _)
        t1 = cnt["T"]

        def _(v, i=i, t1=t1):
            wt(v, "vector", "T", t1)
            for g in range(2):
                ii = v.tensor_copy(
                    xnT.rearrange("p (c t) -> p c t", t=T)[:, 4 * g:4 * g + 4,
                                                           bass.ts(i, 128)],
                    ptb[g][:, 0:512].rearrange("p (c t) -> p c t", t=128))
            inc(ii, "V")
        sec("vector", _)

    def _(g):
        i = g.dma_start(wqp[0].rearrange("p (c o) -> p c o", o=384),
                        d_wqkv[0].rearrange("(c p) o -> c p o", p=128)
                        .rearrange("c p o -> p c o"))
        inc(i, "G")
        inc(g.dma_start(QhA[64:70, :], d_qaug), "G")
        inc(g.dma_start(QhB[64:70, :], d_qaug), "G")
        inc(g.dma_start(KhA[64:70, :], d_kaug[0]), "G")
        i = g.dma_start(KhB[64:70, :], d_kaug[1])
        inc(i, "G")
    sec("gpsimd", _)
    aug_snap = cnt["G"]

    for p in range(4):
        par = p % 2
        wq = wqp[par]
        pair_x_snap = cnt["X"]
        if p + 1 < 4:
            def _(g, p=p):
                wt(g, "gpsimd", "T")
                i = g.dma_start(wqp[(p + 1) % 2].rearrange("p (c o) -> p c o", o=384),
                                d_wqkv[p + 1].rearrange("(c p) o -> c p o", p=128)
                                .rearrange("c p o -> p c o"))
                inc(i, "G")
            sec("gpsimd", _)
        gsnap = aug_snap if p == 0 else cnt["G"]

        for tc in range(8):
            bq = ps["sc0"] if tc % 2 == 0 else ps["sc1"]
            bk = ps["sc2"] if tc % 2 == 0 else ps["sc3"]

            def _(t, tc=tc, bq=bq, bk=bk, gsnap=gsnap):
                wt(t, "tensor", "G", gsnap)
                wt(t, "tensor", "S")
                wt(t, "tensor", "V")
                for fc in range(NFC):
                    ii = t.matmul(bq[:, 0:512], wq[:, 384 * fc:384 * fc + 128],
                                  xnT[:, fc * T + 512 * tc:fc * T + 512 * tc + 512],
                                  start=(fc == 0), stop=(fc == NFC - 1))
                inc(ii, "T")
                for fc in range(NFC):
                    ii = t.matmul(bk[:, 0:512], wq[:, 384 * fc + 128:384 * fc + 256],
                                  xnT[:, fc * T + 512 * tc:fc * T + 512 * tc + 512],
                                  start=(fc == 0), stop=(fc == NFC - 1))
                inc(ii, "T")
                wt(t, "tensor", "V")
                for k in range(4):
                    tt = 4 * tc + k
                    for fc in range(NFC):
                        ii = t.matmul(ps["po"][:, 128 * k:128 * k + 128],
                                      xnT[:, fc * T + 128 * tt:fc * T + 128 * tt + 128],
                                      wq[:, 384 * fc + 256:384 * fc + 384],
                                      start=(fc == 0), stop=(fc == NFC - 1))
                inc(ii, "T")
            sec("tensor", _)
            tsn = cnt["T"]

            def _(s, tc=tc, bq=bq, bk=bk, tsn=tsn):
                ts512 = bass.ts(tc, 512)
                wt(s, "scalar", "T", tsn - 2)
                s.activation(QhA[0:64, ts512], bq[0:64, 0:512], ACT.Identity, scale=0.125)
                ii = s.activation(QhB[0:64, ts512], bq[64:128, 0:512], ACT.Identity,
                                  scale=0.125)
                wt(s, "scalar", "T", tsn - 1)
                s.activation(KhA[0:64, ts512], bk[0:64, 0:512], ACT.Identity)
                ii = s.activation(KhB[0:64, ts512], bk[64:128, 0:512], ACT.Identity)
                inc(ii, "S")
            sec("scalar", _)

            def _(v, tc=tc, tsn=tsn):
                wt(v, "vector", "T", tsn)
                po4 = ps["po"][:, 0:512].rearrange("p (k h d) -> p k h d", k=4, h=2)
                v.tensor_copy(
                    VtA.rearrange("p (t d) -> p t d", d=64)[:, 4 * tc:4 * tc + 4, :],
                    po4[:, :, 0, :])
                ii = v.tensor_copy(
                    VtB.rearrange("p (t d) -> p t d", d=64)[:, 4 * tc:4 * tc + 4, :],
                    po4[:, :, 1, :])
                inc(ii, "V")
            sec("vector", _)

        for lh in range(2):
            Qh = QhA if lh == 0 else QhB
            Kh = KhA if lh == 0 else KhB
            Vt = VtA if lh == 0 else VtB
            orow = 64 * lh
            pstate = {}

            for q in range(NT + 1):
                if q < NT:
                    spar = q % 2
                    P = Pst[spar]
                    nco = 12 * spar
                    nk = q + 1
                    nchunk = (nk + 3) // 4
                    ch_info = []
                    for ci in range(nchunk):
                        k0 = 512 * ci
                        w = min(k0 + 512, 128 * nk) - k0
                        bank = ps["sc0"] if ci % 2 == 0 else ps["sc1"]
                        last = ci == nchunk - 1
                        ch_info.append((k0, w))

                        def _(t, q=q, k0=k0, w=w, bank=bank, last=last):
                            wt(t, "tensor", "S")
                            if last:
                                t.matmul(bank[:, 0:w], Qh[0:70, bass.ts(q, 128)],
                                         Kh[0:70, k0:k0 + w], start=True, stop=False)
                                ii = t.matmul(bank[:, w - 128:w], identS[:, :],
                                              causS[:, :], start=False, stop=True,
                                              skip_group_check=True)
                            else:
                                ii = t.matmul(bank[:, 0:w], Qh[0:70, bass.ts(q, 128)],
                                              Kh[0:70, k0:k0 + w], start=True,
                                              stop=True)
                            inc(ii, "T")
                        sec("tensor", _)
                        tsn = cnt["T"]

                        def _(v, ci=ci, w=w, bank=bank, nco=nco, tsn=tsn):
                            wt(v, "vector", "T", tsn)
                            ii = v.reduce_max(nmS[:, nco + ci:nco + ci + 1],
                                              bank[:, 0:w], axis=AX.X, negate=True)
                            inc(ii, "V")
                        sec("vector", _)
                        vsn = cnt["V"]

                        def _(s, ci=ci, k0=k0, w=w, bank=bank, nco=nco, vsn=vsn,
                              tsn=tsn, P=P):
                            wt(s, "scalar", "V", vsn)
                            wt(s, "scalar", "T", tsn)
                            ii = s.activation(P[:, k0:k0 + w], bank[:, 0:w], ACT.Exp,
                                              bias=nmS[:, nco + ci:nco + ci + 1],
                                              accum_out=lsS[:, nco + ci:nco + ci + 1])
                            inc(ii, "S")
                        sec("scalar", _)
                    ssn_exp = cnt["S"]

                    def _(v, nchunk=nchunk, nco=nco, ssn=ssn_exp):
                        wt(v, "vector", "S", ssn)
                        ii = v.tensor_reduce(CcS[:, nco + 8:nco + 9],
                                             nmS[:, nco:nco + nchunk], axis=AX.X,
                                             op=ALU.min)
                        sv = inc(ii, "V")
                        v.wait_ge(sems["V"], sv)
                        ii = v.tensor_scalar(CcS[:, nco:nco + nchunk],
                                             nmS[:, nco:nco + nchunk],
                                             CcS[:, nco + 8:nco + 9], -1.0,
                                             ALU.subtract, ALU.mult)
                        inc(ii, "V")
                    sec("vector", _)
                    vsn1 = cnt["V"]

                    def _(s, nco=nco, nchunk=nchunk, vsn1=vsn1):
                        wt(s, "scalar", "V", vsn1)
                        ii = s.activation(CcS[:, nco:nco + nchunk],
                                          CcS[:, nco:nco + nchunk], ACT.Exp)
                        inc(ii, "S")
                    sec("scalar", _)
                    ssn1 = cnt["S"]

                    def _(v, nchunk=nchunk, nco=nco, ssn1=ssn1, P=P,
                          ch=tuple(ch_info)):
                        wt(v, "vector", "S", ssn1)
                        ii = v.tensor_mul(lsS[:, nco:nco + nchunk],
                                          lsS[:, nco:nco + nchunk],
                                          CcS[:, nco:nco + nchunk])
                        sv = inc(ii, "V")
                        v.wait_ge(sems["V"], sv)
                        ii = v.reduce_sum(lsS[:, nco + 8:nco + 9],
                                          lsS[:, nco:nco + nchunk], axis=AX.X)
                        sv = inc(ii, "V")
                        v.wait_ge(sems["V"], sv)
                        ii = v.reciprocal(CcS[:, nco + 9:nco + 10],
                                          lsS[:, nco + 8:nco + 9])
                        sv = inc(ii, "V")
                        v.wait_ge(sems["V"], sv)
                        ii = v.tensor_scalar(CcS[:, nco:nco + nchunk],
                                             CcS[:, nco:nco + nchunk],
                                             CcS[:, nco + 9:nco + 10], None, ALU.mult)
                        sv = inc(ii, "V")
                        v.wait_ge(sems["V"], sv)
                        for ci, (k0, w) in enumerate(ch):
                            ii = v.tensor_scalar(P[:, k0:k0 + w], P[:, k0:k0 + w],
                                                 CcS[:, nco + ci:nco + ci + 1],
                                                 None, ALU.mult)
                        inc(ii, "V")
                    sec("vector", _)
                    pstate[q] = cnt["V"]

                if q >= 1:
                    qq = q - 1
                    spar = qq % 2
                    P = Pst[spar]
                    pT = pTs[spar]
                    nk = qq + 1
                    vP = pstate.pop(qq)
                    ngrp = (nk + 3) // 4
                    for gi in range(ngrp):
                        kt0 = 4 * gi
                        ktn = min(4, nk - kt0)
                        pt = ptb[gi % 2]

                        def _(t, kt0=kt0, ktn=ktn, pt=pt, vP=vP, P=P):
                            wt(t, "tensor", "V", vP)
                            wt(t, "tensor", "S")
                            for k in range(ktn):
                                ii = t.transpose(
                                    pt[:, 128 * k:128 * k + 128],
                                    P[:, 128 * (kt0 + k):128 * (kt0 + k) + 128],
                                    identS[:, :])
                            inc(ii, "T")
                        sec("tensor", _)
                        tsn = cnt["T"]

                        def _(s, kt0=kt0, ktn=ktn, pt=pt, tsn=tsn, pT=pT):
                            wt(s, "scalar", "T", tsn)
                            ii = s.activation(pT[:, 128 * kt0:128 * (kt0 + ktn)],
                                              pt[:, 0:128 * ktn], ACT.Identity)
                            inc(ii, "S")
                        sec("scalar", _)
                        ssn = cnt["S"]

                        def _(t, qq=qq, kt0=kt0, ktn=ktn, ssn=ssn, pT=pT, Vt=Vt,
                              nk=nk):
                            wt(t, "tensor", "S", ssn)
                            oreg = ps["po"][0:64, 256 * (qq % 2):256 * (qq % 2) + 128]
                            for k in range(ktn):
                                kt = kt0 + k
                                ii = t.matmul(oreg, Vt[:, 64 * kt:64 * kt + 64],
                                              pT[:, 128 * kt:128 * kt + 128],
                                              start=(kt == 0), stop=(kt == nk - 1),
                                              skip_group_check=True)
                            inc(ii, "T")
                        sec("tensor", _)
                    tav = cnt["T"]

                    def _(s, qq=qq, tav=tav, orow=orow, par=par,
                          pxs=pair_x_snap):
                        wt(s, "scalar", "T", tav)
                        wt(s, "scalar", "X", pxs)
                        oreg = ps["po"][0:64, 256 * (qq % 2):256 * (qq % 2) + 128]
                        ii = s.activation(O2T[par][orow:orow + 64, bass.ts(qq, 128)],
                                          oreg, ACT.Identity)
                        inc(ii, "S")
                    sec("scalar", _)

        def _(sy, p=p, par=par):
            wt(sy, "sync", "S")
            ii = sy.dma_start(d_agin.rearrange("r p t -> (r p) t")[
                bass.ts(p, 128), :], O2T[par][:, :])
            inc(ii, "X")
        sec("sync", _)

        if p + 1 < 4:
            def _(g, p=p):
                wt(g, "gpsimd", "T")
                inc(g.dma_start(KhA[64:70, :], d_kaug[2 * (p + 1)]), "G")
                i = g.dma_start(KhB[64:70, :], d_kaug[2 * (p + 1) + 1])
                inc(i, "G")
            sec("gpsimd", _)

    x_exch = cnt["X"]

    def _(g):
        wt(g, "gpsimd", "X", x_exch)
        g.collective_compute(
            "AllGather", ALU.bypass, replica_groups=[list(range(NCN))],
            ins=[d_agin], outs=[d_ysh],
        ).then_inc(sems["G"], 1)
        g.wait_ge(sems["G"], cnt["G"] + 1)
    sec("gpsimd", _)
    cnt["G"] += 1

    def _(g):
        inc(g.dma_start(rbS[:, :], d_rb), "G")
    sec("gpsimd", _)
    bar = cnt["G"]
    for en in ["tensor", "vector", "scalar", "sync"]:
        def _(e, en=en):
            wt(e, en, "G", bar)
        sec(en, _)

    for name in reversed(["xnT", "QhA", "QhB", "KhA", "KhB", "VtA", "VtB",
                          "Pst0", "Pst1", "pTs0", "pTs1", "O2T0", "O2T1",
                          "wqp0", "wqp1", "xts0", "xts1", "xns0", "xns1",
                          "sqa", "astat", "nmS", "lsS", "CcS"]):
        sfree(name)

    a1T = salloc("a1T", 128, NFC * R, BF16)
    hT = salloc("hT", 128, NFC * R, BF16)
    onehotT = salloc("onehotT", 8, R, BF16)
    bvb = salloc("bvb", 128, 8, BF16)
    wstr = [salloc(f"wstr{i}", 128, C, BF16) for i in range(4)]
    xslt = [salloc(f"xslt{i}", 128, C, F32) for i in range(2)]
    xslb = [salloc(f"xslb{i}", 128, C, BF16) for i in range(2)]
    sqB = salloc("sqB", 128, 4096, BF16)
    xAT = salloc("xAT", 128, NFC * R, BF16)
    ybT = salloc("ybT", 128, NFC * R, BF16)

    def _(sy):
        for bb in range(B):
            src = d_ysh[8 * bb:8 * bb + 8, :, bass.ts(pid, 512)].rearrange(
                "c p j -> p c j")
            dst = ybT.rearrange("p (c b j) -> p c b j", c=NFC, b=B)[:, :, bb, :]
            inc(sy.dma_start(dst, src), "X")
    sec("sync", _)
    ysnap = cnt["X"]

    def load_wchunk(dram4, idx, slot):
        def _(g):
            wt(g, "gpsimd", "T")
            i = g.dma_start(slot.rearrange("p (c o) -> p c o", o=128),
                            dram4[idx].rearrange("(c p) o -> c p o", p=128)
                            .rearrange("c p o -> p c o"))
            inc(i, "G")
        sec("gpsimd", _)
        return cnt["G"]

    wsn = {}
    for oc in range(4):
        wsn[oc] = load_wchunk(d_wproj, oc, wstr[oc])
    for oc in range(NFC):
        if oc >= 4:
            wsn[oc] = load_wchunk(d_wproj, oc, wstr[oc % 4])
        for rc in range(4):
            bank = ps["sc0"] if (rc + oc * 4) % 2 == 0 else ps["sc1"]

            def _(t, rc=rc, oc=oc, bank=bank):
                wt(t, "tensor", "G", wsn[oc])
                wt(t, "tensor", "X", ysnap)
                wt(t, "tensor", "S")
                for fc in range(NFC):
                    rhs = ybT.rearrange("p (c b j) -> p c b j", c=NFC, b=B)[
                        :, fc, :, bass.ts(rc, 128)].rearrange("p b j -> p j b")
                    ii = t.matmul(bank[:, 0:512],
                                  wstr[oc % 4][:, 128 * fc:128 * fc + 128],
                                  rhs, start=(fc == 0), stop=(fc == NFC - 1))
                inc(ii, "T")
            sec("tensor", _)
            tsn = cnt["T"]

            def _(s, rc=rc, oc=oc, bank=bank, tsn=tsn):
                wt(s, "scalar", "T", tsn)
                ii = s.activation(a1T[:, oc * R + 512 * rc:oc * R + 512 * rc + 512],
                                  bank[:, 0:512], ACT.Identity)
                inc(ii, "S")
            sec("scalar", _)
    a1_done = cnt["S"]

    for rt in range(16):
        par = rt % 2

        def _(sy, rt=rt, par=par):
            if rt >= 2:
                wt(sy, "sync", "S")
            ii = sy.dma_start(xslt[par][:, :],
                              d_xsl.rearrange("b j f -> j b f")[bass.ts(rt, 32), :, :])
            inc(ii, "X")
        sec("sync", _)
        xsn = cnt["X"]

        def _(s, par=par, xsn=xsn, rt=rt):
            wt(s, "scalar", "X", xsn)
            if rt >= 2:
                wt(s, "scalar", "T")
            ii = s.activation(xslb[par][:, :], xslt[par][:, :], ACT.Identity)
            inc(ii, "S")
        sec("scalar", _)
        ssn = cnt["S"]

        def _(t, par=par, ssn=ssn, rt=rt):
            wt(t, "tensor", "S", ssn)
            wt(t, "tensor", "V")
            for g in range(2):
                for k in range(4):
                    fc = 4 * g + k
                    ii = t.transpose(ptb[g][:, 128 * k:128 * k + 128],
                                     xslb[par][:, 128 * fc:128 * fc + 128],
                                     identS[:, :])
                inc(ii, "T")
        sec("tensor", _)
        tsn = cnt["T"]

        def _(v, rt=rt, tsn=tsn):
            wt(v, "vector", "T", tsn)
            wt(v, "vector", "S", a1_done)
            for g in range(2):
                ii = v.tensor_tensor(
                    xAT.rearrange("p (c r) -> p c r", r=R)[:, 4 * g:4 * g + 4,
                                                           bass.ts(rt, 128)],
                    ptb[g][:, 0:512].rearrange("p (c r) -> p c r", r=128),
                    a1T.rearrange("p (c r) -> p c r", r=R)[:, 4 * g:4 * g + 4,
                                                           bass.ts(rt, 128)],
                    ALU.add)
            inc(ii, "V")
        sec("vector", _)
    sfree("ybT")

    def ln_fm(src, dst, gc, bc):
        for rc in range(4):
            def _(s, rc=rc):
                wt(s, "scalar", "V")
                for fc in range(NFC):
                    ii = s.activation(sqB[:, 512 * fc:512 * fc + 512],
                                      src[:, fc * R + 512 * rc:fc * R + 512 * rc + 512],
                                      ACT.Square)
                inc(ii, "S")
            sec("scalar", _)
            ssn = cnt["S"]

            def _(t, rc=rc, ssn=ssn):
                wt(t, "tensor", "S", ssn)
                wt(t, "tensor", "V")
                for fc in range(NFC):
                    ii = t.matmul(ps["pe"][0:1, 0:512], onescS[:, :],
                                  src[:, fc * R + 512 * rc:fc * R + 512 * rc + 512],
                                  start=(fc == 0), stop=(fc == NFC - 1))
                inc(ii, "T")
                for fc in range(NFC):
                    ii = t.matmul(ps["po"][0:1, 0:512], onescS[:, :],
                                  sqB[:, 512 * fc:512 * fc + 512],
                                  start=(fc == 0), stop=(fc == NFC - 1))
                inc(ii, "T")
            sec("tensor", _)
            tsn = cnt["T"]

            def _(v, tsn=tsn):
                wt(v, "vector", "T", tsn)
                v.tensor_copy(statrow[0:1, 0:512], ps["pe"][0:1, 0:512])
                ii = v.tensor_copy(statrow[0:1, 512:1024], ps["po"][0:1, 0:512])
                sv = inc(ii, "V")
                v.wait_ge(sems["V"], sv)
                ii = v.tensor_mul(statrow[0:1, 1024:1536], statrow[0:1, 0:512],
                                  statrow[0:1, 0:512])
                sv = inc(ii, "V")
                v.wait_ge(sems["V"], sv)
                ii = v.tensor_sub(statrow[0:1, 512:1024], statrow[0:1, 512:1024],
                                  statrow[0:1, 1024:1536])
                inc(ii, "V")
            sec("vector", _)
            vsn = cnt["V"]

            def _(s, vsn=vsn):
                wt(s, "scalar", "V", vsn)
                ii = s.activation(statrow[0:1, 1536:2048], statrow[0:1, 512:1024],
                                  ACT.Sqrt, bias=epsc[0:1, 0:1])
                inc(ii, "S")
            sec("scalar", _)
            ssn2a = cnt["S"]

            def _(v, ssn2a=ssn2a):
                wt(v, "vector", "S", ssn2a)
                ii = v.reciprocal(statrow[0:1, 1024:1536], statrow[0:1, 1536:2048])
                inc(ii, "V")
            sec("vector", _)
            ssn2 = cnt["S"]
            vsn_r = cnt["V"]

            def _(t, ssn2=ssn2, vsn_r=vsn_r):
                wt(t, "tensor", "S", ssn2)
                wt(t, "tensor", "V", vsn_r)
                wt(t, "tensor", "V")
                ii = t.matmul(ps["sc2"][:, 0:512], ones1fS[:, :],
                              statrow[0:1, 0:512], start=True, stop=True)
                inc(ii, "T")
                ii = t.matmul(ps["sc3"][:, 0:512], ones1fS[:, :],
                              statrow[0:1, 1024:1536], start=True, stop=True)
                inc(ii, "T")
            sec("tensor", _)
            tsn2 = cnt["T"]

            def _(v, rc=rc, tsn2=tsn2):
                wt(v, "vector", "T", tsn2)
                for fc in range(NFC):
                    ii = v.tensor_sub(sqB[:, 512 * fc:512 * fc + 512],
                                      src[:, fc * R + 512 * rc:fc * R + 512 * rc + 512],
                                      ps["sc2"][:, 0:512])
                sv = inc(ii, "V")
                v.wait_ge(sems["V"], sv)
                for fc in range(NFC):
                    ii = v.tensor_mul(sqB[:, 512 * fc:512 * fc + 512],
                                      sqB[:, 512 * fc:512 * fc + 512],
                                      ps["sc3"][:, 0:512])
                inc(ii, "V")
            sec("vector", _)
            vsn2 = cnt["V"]

            def _(s, rc=rc, vsn2=vsn2):
                wt(s, "scalar", "V", vsn2)
                for fc in range(NFC):
                    if gc is None:
                        ii = s.activation(
                            dst[:, fc * R + 512 * rc:fc * R + 512 * rc + 512],
                            sqB[:, 512 * fc:512 * fc + 512], ACT.Identity)
                    else:
                        ii = s.activation(
                            dst[:, fc * R + 512 * rc:fc * R + 512 * rc + 512],
                            sqB[:, 512 * fc:512 * fc + 512], ACT.Identity,
                            scale=gc[:, fc:fc + 1], bias=bc[:, fc:fc + 1])
                inc(ii, "S")
            sec("scalar", _)

    ln_fm(xAT, xAT, None, None)

    ctxT = salloc("ctxT", 128, NFC * R, BF16)
    v2c = [salloc(f"v2c{i}", 128, 2048, BF16) for i in range(2)]
    q2c = [salloc(f"q2c{i}", 128, 2048, BF16) for i in range(2)]
    k2c = [salloc(f"k2c{i}", 128, 2048, BF16) for i in range(2)]
    sstat = salloc("sstat", 128, 8, F32)
    Pb = [salloc(f"Pb{i}", 128, 128, BF16) for i in range(2)]
    pTb = [salloc(f"pTb{i}", 128, 128, BF16) for i in range(2)]

    for oc in range(NFC):
        par = oc % 2
        sq = wstr[(3 * oc) % 4]
        sk = wstr[(3 * oc + 1) % 4]
        sv_ = wstr[(3 * oc + 2) % 4]
        gq = load_wchunk(d_win, oc, sq)
        gk = load_wchunk(d_win, 8 + oc, sk)
        gv = load_wchunk(d_win, 16 + oc, sv_)

        for part, slot, gsn, dst in ((0, sq, gq, q2c[par]), (1, sk, gk, k2c[par])):
            for rc in range(4):
                bank = ps["sc0"] if rc % 2 == 0 else ps["sc1"]

                def _(t, slot=slot, gsn=gsn, rc=rc, bank=bank):
                    wt(t, "tensor", "G", gsn)
                    wt(t, "tensor", "S")
                    for fc in range(NFC):
                        ii = t.matmul(bank[:, 0:512],
                                      slot[:, 128 * fc:128 * fc + 128],
                                      xAT[:, fc * R + 512 * rc:fc * R + 512 * rc + 512],
                                      start=(fc == 0), stop=(fc == NFC - 1))
                    inc(ii, "T")
                sec("tensor", _)
                tsn = cnt["T"]

                def _(s, part=part, oc=oc, rc=rc, bank=bank, tsn=tsn, dst=dst):
                    wt(s, "scalar", "T", tsn)
                    if part == 0:
                        ii = s.activation(dst[:, 512 * rc:512 * rc + 512],
                                          bank[:, 0:512], ACT.Identity, scale=0.125,
                                          bias=inbcq[:, oc:oc + 1])
                    else:
                        ii = s.activation(dst[:, 512 * rc:512 * rc + 512],
                                          bank[:, 0:512], ACT.Identity,
                                          bias=inbc[:, 8 + oc:9 + oc])
                    inc(ii, "S")
                sec("scalar", _)

        for rt in range(16):
            bank = ps["sc0"] if rt % 2 == 0 else ps["sc1"]

            def _(t, gv=gv, sv_=sv_, rt=rt, bank=bank):
                wt(t, "tensor", "G", gv)
                wt(t, "tensor", "V")
                wt(t, "tensor", "S")
                for fc in range(NFC):
                    ii = t.matmul(bank[:, 0:128],
                                  xAT[:, fc * R + 128 * rt:fc * R + 128 * rt + 128],
                                  sv_[:, 128 * fc:128 * fc + 128],
                                  start=(fc == 0), stop=(fc == NFC - 1))
                inc(ii, "T")
            sec("tensor", _)
            tsn = cnt["T"]

            def _(v, par=par, rt=rt, bank=bank, tsn=tsn):
                wt(v, "vector", "T", tsn)
                ii = v.tensor_copy(v2c[par][:, 128 * rt:128 * rt + 128],
                                   bank[:, 0:128])
                inc(ii, "V")
            sec("vector", _)

        for lh in range(2):
            h = 2 * oc + lh
            qc = oc
            hrow = 64 * lh
            for rt in range(16):
                it = (lh * 16 + rt) % 2
                bank = ps["sc0"] if it == 0 else ps["sc1"]
                so = 4 * it

                def _(t, par=par, hrow=hrow, rt=rt, bank=bank):
                    wt(t, "tensor", "S")
                    wt(t, "tensor", "V")
                    t.matmul(bank[:, 0:128],
                             q2c[par][hrow:hrow + 64, 128 * rt:128 * rt + 128],
                             k2c[par][hrow:hrow + 64, 128 * rt:128 * rt + 128],
                             start=True, stop=False)
                    ii = t.matmul(bank[:, 0:128], identS[:, :], bdmS[:, :],
                                  start=False, stop=True, skip_group_check=True)
                    inc(ii, "T")
                sec("tensor", _)
                tsn = cnt["T"]

                def _(v, bank=bank, so=so, tsn=tsn):
                    wt(v, "vector", "T", tsn)
                    ii = v.reduce_max(sstat[:, so:so + 1], bank[:, 0:128], axis=AX.X,
                                      negate=True)
                    inc(ii, "V")
                sec("vector", _)
                vsn = cnt["V"]

                def _(s, bank=bank, so=so, vsn=vsn, it=it):
                    wt(s, "scalar", "V", vsn)
                    wt(s, "scalar", "T")
                    ii = s.activation(Pb[it][:, :], bank[:, 0:128], ACT.Exp,
                                      bias=sstat[:, so:so + 1],
                                      accum_out=sstat[:, so + 1:so + 2])
                    inc(ii, "S")
                sec("scalar", _)
                ssn = cnt["S"]

                def _(v, so=so, ssn=ssn, it=it):
                    wt(v, "vector", "S", ssn)
                    ii = v.reciprocal(sstat[:, so + 2:so + 3], sstat[:, so + 1:so + 2])
                    sv = inc(ii, "V")
                    v.wait_ge(sems["V"], sv)
                    ii = v.tensor_scalar(Pb[it][:, :], Pb[it][:, :],
                                         sstat[:, so + 2:so + 3], None, ALU.mult)
                    inc(ii, "V")
                sec("vector", _)
                vsn2 = cnt["V"]

                def _(t, vsn2=vsn2, it=it):
                    wt(t, "tensor", "V", vsn2)
                    wt(t, "tensor", "S")
                    ii = t.transpose(ptb[it][:, 0:128], Pb[it][:, :], identS[:, :])
                    inc(ii, "T")
                sec("tensor", _)
                tsn2 = cnt["T"]

                def _(s, tsn2=tsn2, it=it):
                    wt(s, "scalar", "T", tsn2)
                    ii = s.activation(pTb[it][:, :], ptb[it][:, 0:128], ACT.Identity)
                    inc(ii, "S")
                sec("scalar", _)
                ssn2 = cnt["S"]

                def _(t, par=par, lh=lh, rt=rt, ssn2=ssn2, it=it):
                    wt(t, "tensor", "S", ssn2)
                    ii = t.matmul(ps["po"][0:64, 128 * it:128 * it + 128],
                                  v2c[par][:, 128 * rt + 64 * lh:128 * rt + 64 * lh + 64],
                                  pTb[it][:, :], start=True, stop=True,
                                  skip_group_check=True)
                    inc(ii, "T")
                sec("tensor", _)
                tsn3 = cnt["T"]

                def _(s, qc=qc, hrow=hrow, rt=rt, tsn3=tsn3, it=it):
                    wt(s, "scalar", "T", tsn3)
                    ii = s.activation(
                        ctxT[hrow:hrow + 64, qc * R + 128 * rt:qc * R + 128 * rt + 128],
                        ps["po"][0:64, 128 * it:128 * it + 128], ACT.Identity)
                    inc(ii, "S")
                sec("scalar", _)

    sfree("pTb1"); sfree("pTb0"); sfree("Pb1"); sfree("Pb0"); sfree("sstat")
    sfree("k2c1"); sfree("k2c0"); sfree("q2c1"); sfree("q2c0")
    sfree("v2c1"); sfree("v2c0")

    def _(v):
        ii = v.tensor_copy(bvb[:, :], inbc[:, 16:24])
        inc(ii, "V")
    sec("vector", _)

    wsn3 = {}
    for oc in range(4):
        wsn3[oc] = load_wchunk(d_wout, oc, wstr[oc])
    for oc in range(NFC):
        if oc >= 4:
            wsn3[oc] = load_wchunk(d_wout, oc, wstr[oc % 4])

        def _(t, oc=oc):
            wt(t, "tensor", "G", wsn3[oc])
            wt(t, "tensor", "V")
            for fc in range(NFC):
                ii = t.matmul(ps["pe"][:, 32 + oc:32 + oc + 1],
                              wstr[oc % 4][:, 128 * fc:128 * fc + 128],
                              bvb[:, fc:fc + 1],
                              start=(fc == 0), stop=(fc == NFC - 1))
            inc(ii, "T")
        sec("tensor", _)
        tbc = cnt["T"]

        def _(v, oc=oc, tbc=tbc):
            wt(v, "vector", "T", tbc)
            ii = v.tensor_add(bcc[:, oc:oc + 1], ps["pe"][:, 32 + oc:32 + oc + 1],
                              outbc[:, oc:oc + 1])
            inc(ii, "V")
        sec("vector", _)
        bcc_done = cnt["V"]

        for rc in range(4):
            bank = ps["sc0"] if (rc + oc * 4) % 2 == 0 else ps["sc1"]

            def _(t, rc=rc, oc=oc, bank=bank):
                wt(t, "tensor", "G", wsn3[oc])
                wt(t, "tensor", "S")
                for fc in range(NFC):
                    ii = t.matmul(bank[:, 0:512],
                                  wstr[oc % 4][:, 128 * fc:128 * fc + 128],
                                  ctxT[:, fc * R + 512 * rc:fc * R + 512 * rc + 512],
                                  start=(fc == 0), stop=(fc == NFC - 1))
                inc(ii, "T")
            sec("tensor", _)
            tsn = cnt["T"]

            def _(s, rc=rc, oc=oc, bank=bank, tsn=tsn, bcc_done=bcc_done):
                wt(s, "scalar", "T", tsn)
                wt(s, "scalar", "V", bcc_done)
                ii = s.activation(hT[:, oc * R + 512 * rc:oc * R + 512 * rc + 512],
                                  bank[:, 0:512], ACT.Identity, bias=bcc[:, oc:oc + 1])
                inc(ii, "S")
            sec("scalar", _)
            ssn = cnt["S"]

            def _(v, rc=rc, oc=oc, ssn=ssn):
                wt(v, "vector", "S", ssn)
                sl = slice(oc * R + 512 * rc, oc * R + 512 * rc + 512)
                ii = v.tensor_add(hT[:, sl], hT[:, sl], xAT[:, sl])
                inc(ii, "V")
            sec("vector", _)

    sfree("ctxT")
    sfree("xAT")

    ln_fm(hT, hT, g1c, bb1c)

    lgS = salloc("lgS", 128, 8, F32)
    lg2S = salloc("lg2S", 128, 8, F32)
    topS = salloc("topS", 128, 8, F32)
    ohS = salloc("ohS", 128, 8, BF16)
    for rt in range(16):
        def _(t, rt=rt):
            wt(t, "tensor", "V")
            wt(t, "tensor", "S")
            for fc in range(NFC):
                t.matmul(ps["pe"][:, 0:8], hT[:, fc * R + 128 * rt:fc * R + 128 * rt + 128],
                         rwS[:, 8 * fc:8 * fc + 8], start=(fc == 0), stop=False)
            ii = t.matmul(ps["pe"][:, 0:8], ones1S[:, :], rbS[:, :],
                          start=False, stop=True, skip_group_check=True)
            inc(ii, "T")
        sec("tensor", _)
        tsn = cnt["T"]

        def _(v, rt=rt, tsn=tsn):
            BIG = 65536.0
            wt(v, "vector", "T", tsn)
            ii = v.tensor_copy(lgS[:, :], ps["pe"][:, 0:8])
            sv = inc(ii, "V")
            v.wait_ge(sems["V"], sv)
            ii = v.reduce_max(topS[:, 0:1], lgS[:, :], axis=AX.X)
            sv = inc(ii, "V")
            v.wait_ge(sems["V"], sv)
            ii = v.tensor_scalar(lg2S[:, :], lgS[:, :], topS[:, 0:1], None,
                                 ALU.is_equal)
            sv = inc(ii, "V")
            v.wait_ge(sems["V"], sv)
            ii = v.scalar_tensor_tensor(lg2S[:, :], lg2S[:, :], -BIG, iota8S[:, :],
                                        ALU.mult, ALU.add)
            sv = inc(ii, "V")
            v.wait_ge(sems["V"], sv)
            ii = v.tensor_scalar(lg2S[:, :], lg2S[:, :], BIG, None, ALU.add)
            sv = inc(ii, "V")
            v.wait_ge(sems["V"], sv)
            ii = v.tensor_reduce(topS[:, 1:2], lg2S[:, :], axis=AX.X, op=ALU.min)
            sv = inc(ii, "V")
            v.wait_ge(sems["V"], sv)
            ii = v.tensor_scalar(lg2S[:, :], iota8S[:, :], topS[:, 1:2], None,
                                 ALU.is_equal)
            sv = inc(ii, "V")
            v.wait_ge(sems["V"], sv)
            ii = v.scalar_tensor_tensor(lg2S[:, :], lg2S[:, :], -BIG, lgS[:, :],
                                        ALU.mult, ALU.add)
            sv = inc(ii, "V")
            v.wait_ge(sems["V"], sv)
            ii = v.reduce_max(topS[:, 2:3], lg2S[:, :], axis=AX.X)
            sv = inc(ii, "V")
            v.wait_ge(sems["V"], sv)
            ii = v.tensor_scalar(lg2S[:, :], lg2S[:, :], topS[:, 2:3], None,
                                 ALU.is_equal)
            sv = inc(ii, "V")
            v.wait_ge(sems["V"], sv)
            ii = v.scalar_tensor_tensor(lg2S[:, :], lg2S[:, :], -BIG, iota8S[:, :],
                                        ALU.mult, ALU.add)
            sv = inc(ii, "V")
            v.wait_ge(sems["V"], sv)
            ii = v.tensor_scalar(lg2S[:, :], lg2S[:, :], BIG, None, ALU.add)
            sv = inc(ii, "V")
            v.wait_ge(sems["V"], sv)
            ii = v.tensor_reduce(topS[:, 3:4], lg2S[:, :], axis=AX.X, op=ALU.min)
            sv = inc(ii, "V")
            v.wait_ge(sems["V"], sv)
            ii = v.tensor_max(topS[:, 4:5], topS[:, 1:2], topS[:, 3:4])
            sv = inc(ii, "V")
            v.wait_ge(sems["V"], sv)
            ii = v.tensor_scalar(ohS[:, :], iota8S[:, :], topS[:, 4:5], None,
                                 ALU.is_equal)
            inc(ii, "V")
        sec("vector", _)
        vsn = cnt["V"]

        def _(t, vsn=vsn):
            wt(t, "tensor", "V", vsn)
            wt(t, "tensor", "S")
            ii = t.transpose(pt0[0:8, 0:128], ohS[:, :], identS[:, :])
            inc(ii, "T")
        sec("tensor", _)
        tsn2 = cnt["T"]

        def _(s, rt=rt, tsn2=tsn2):
            wt(s, "scalar", "T", tsn2)
            ii = s.activation(onehotT[0:8, bass.ts(rt, 128)], pt0[0:8, 0:128],
                              ACT.Identity)
            inc(ii, "S")
        sec("scalar", _)

    sfree("ohS"); sfree("topS"); sfree("lg2S"); sfree("lgS")

    out2T = salloc("out2T", 128, NFC * R, BF16)
    heT = salloc("heT", 128, NFC * R, BF16)
    ohrow = salloc("ohrow", 1, R, BF16)
    maskB = salloc("maskB", 128, R, BF16)
    w1r = [salloc(f"w1r{i}", 128, C, BF16) for i in range(4)]
    w2r = [salloc(f"w2r{i}", 128, C, BF16) for i in range(4)]

    oh_done = cnt["S"]
    for f2c in range(NFC):
        for rc in range(4):
            bank = ps["sc0"] if (f2c * 4 + rc) % 2 == 0 else ps["sc1"]

            def _(t, f2c=f2c, rc=rc, bank=bank):
                wt(t, "tensor", "S", oh_done)
                wt(t, "tensor", "S")
                ii = t.matmul(bank[:, 0:512], b2S[0:8, 128 * f2c:128 * f2c + 128],
                              onehotT[0:8, 512 * rc:512 * rc + 512],
                              start=True, stop=True)
                inc(ii, "T")
            sec("tensor", _)
            tsn = cnt["T"]

            def _(s, f2c=f2c, rc=rc, bank=bank, tsn=tsn):
                wt(s, "scalar", "T", tsn)
                ii = s.activation(out2T[:, f2c * R + 512 * rc:f2c * R + 512 * rc + 512],
                                  bank[:, 0:512], ACT.Identity)
                inc(ii, "S")
            sec("scalar", _)

    def load_moe(dram, e, idx, slot):
        def _(g):
            wt(g, "gpsimd", "T")
            i = g.dma_start(slot.rearrange("p (c o) -> p c o", o=128),
                            dram[e, idx].rearrange("(c p) o -> c p o", p=128)
                            .rearrange("c p o -> p c o"))
            inc(i, "G")
        sec("gpsimd", _)
        return cnt["G"]

    for e in range(E):
        def _(sy, e=e):
            wt(sy, "sync", "S")
            wt(sy, "sync", "T")
            ii = sy.dma_start(ohrow[0:1, :], onehotT[e:e + 1, :])
            inc(ii, "X")
        sec("sync", _)
        ohx = cnt["X"]

        for rc in range(4):
            breg = ps["pe"] if rc % 2 == 0 else ps["po"]

            def _(t, e=e, rc=rc, breg=breg, ohx=ohx):
                wt(t, "tensor", "X", ohx)
                wt(t, "tensor", "V")
                ii = t.matmul(breg[:, 0:512], ones1S[:, :],
                              ohrow[0:1, 512 * rc:512 * rc + 512],
                              start=True, stop=True, skip_group_check=True)
                inc(ii, "T")
            sec("tensor", _)
            tsn = cnt["T"]

            def _(v, e=e, rc=rc, breg=breg, tsn=tsn):
                wt(v, "vector", "T", tsn)
                wt(v, "vector", "S")
                ii = v.tensor_copy(maskB[:, 512 * rc:512 * rc + 512], breg[:, 0:512])
                inc(ii, "V")
            sec("vector", _)
        mask_done = cnt["V"]

        for oc in range(NFC):
            slot = w1r[oc % 4]
            gs = load_moe(d_w1, e, oc, slot)
            for rc in range(4):
                bank = ps["sc0"] if (oc * 4 + rc) % 2 == 0 else ps["sc1"]

                def _(t, oc=oc, rc=rc, bank=bank, gs=gs, slot=slot):
                    wt(t, "tensor", "G", gs)
                    wt(t, "tensor", "V")
                    wt(t, "tensor", "S")
                    for fc in range(NFC):
                        ii = t.matmul(bank[:, 0:512],
                                      slot[:, 128 * fc:128 * fc + 128],
                                      hT[:, fc * R + 512 * rc:fc * R + 512 * rc + 512],
                                      start=(fc == 0), stop=(fc == NFC - 1))
                    inc(ii, "T")
                sec("tensor", _)
                tsn = cnt["T"]

                def _(s, e=e, oc=oc, rc=rc, bank=bank, tsn=tsn):
                    wt(s, "scalar", "T", tsn)
                    wt(s, "scalar", "V")
                    ii = s.activation(
                        heT[:, oc * R + 512 * rc:oc * R + 512 * rc + 512],
                        bank[:, 0:512], ACT.Gelu, bias=b1c[:, 8 * e + oc:8 * e + oc + 1])
                    inc(ii, "S")
                sec("scalar", _)
                ssn = cnt["S"]

                def _(v, oc=oc, rc=rc, ssn=ssn, mask_done=mask_done):
                    wt(v, "vector", "S", ssn)
                    wt(v, "vector", "V", mask_done)
                    sl = slice(oc * R + 512 * rc, oc * R + 512 * rc + 512)
                    ii = v.tensor_mul(heT[:, sl], heT[:, sl],
                                      maskB[:, 512 * rc:512 * rc + 512])
                    inc(ii, "V")
                sec("vector", _)
        he_done = cnt["V"]

        for f2c in range(NFC):
            slot = w2r[f2c % 4]
            gs = load_moe(d_w2, e, f2c, slot)
            for rc in range(4):
                bank = ps["sc2"] if (f2c * 4 + rc) % 2 == 0 else ps["sc3"]

                def _(t, f2c=f2c, rc=rc, bank=bank, gs=gs, slot=slot,
                      he_done=he_done):
                    wt(t, "tensor", "G", gs)
                    wt(t, "tensor", "V", he_done)
                    wt(t, "tensor", "V")
                    for hc in range(NFC):
                        ii = t.matmul(bank[:, 0:512],
                                      slot[:, 128 * hc:128 * hc + 128],
                                      heT[:, hc * R + 512 * rc:hc * R + 512 * rc + 512],
                                      start=(hc == 0), stop=(hc == NFC - 1))
                    inc(ii, "T")
                sec("tensor", _)
                tsn = cnt["T"]

                def _(v, f2c=f2c, rc=rc, bank=bank, tsn=tsn):
                    wt(v, "vector", "T", tsn)
                    wt(v, "vector", "S")
                    wt(v, "vector", "V")
                    sl = slice(f2c * R + 512 * rc, f2c * R + 512 * rc + 512)
                    ii = v.tensor_add(out2T[:, sl], out2T[:, sl], bank[:, 0:512])
                    inc(ii, "V")
                sec("vector", _)

    for i in (3, 2, 1, 0):
        sfree(f"w2r{i}")
    for i in (3, 2, 1, 0):
        sfree(f"w1r{i}")
    sfree("maskB")

    moe_done = cnt["V"]

    def _(v, moe_done=moe_done):
        wt(v, "vector", "V", moe_done)
        for fc in range(NFC):
            for rc in range(4):
                sl = slice(fc * R + 512 * rc, fc * R + 512 * rc + 512)
                ii = v.tensor_add(heT[:, sl], hT[:, sl], out2T[:, sl])
        inc(ii, "V")
    sec("vector", _)
    lnin_done = cnt["V"]

    def _(s, lnin_done=lnin_done):
        wt(s, "scalar", "V", lnin_done)
    sec("scalar", _)

    ffnT = out2T
    ln_fm(heT, ffnT, g2c, bb2c)
    ffn_done = cnt["S"]

    outb2 = [salloc(f"outb2{i}", 128, C, F32) for i in range(2)]
    for rt in range(16):
        par = rt % 2

        def _(sy, rt=rt, par=par):
            if rt >= 2:
                wt(sy, "sync", "V")
            ii = sy.dma_start(xslt[par][:, :],
                              d_xsl.rearrange("b j f -> j b f")[bass.ts(rt, 32), :, :])
            inc(ii, "X")
        sec("sync", _)
        xsn = cnt["X"]

        def _(t, rt=rt, ffn_done=ffn_done):
            wt(t, "tensor", "S", ffn_done)
            wt(t, "tensor", "S")
            wt(t, "tensor", "V")
            for g in range(2):
                for k in range(4):
                    fc = 4 * g + k
                    ii = t.transpose(ptb[g][:, 128 * k:128 * k + 128],
                                     a1T[:, fc * R + 128 * rt:fc * R + 128 * rt + 128],
                                     identS[:, :])
                inc(ii, "T")
        sec("tensor", _)
        t_a1 = cnt["T"]

        def _(s, rt=rt, par=par, t_a1=t_a1):
            wt(s, "scalar", "T", t_a1)
            ii = s.activation(xslb[par][:, 0:512], ptb[0][:, 0:512], ACT.Identity)
            ii = s.activation(xslb[par][:, 512:1024], ptb[1][:, 0:512], ACT.Identity)
            inc(ii, "S")
        sec("scalar", _)
        s_a1 = cnt["S"]

        def _(t, rt=rt, s_a1=s_a1):
            wt(t, "tensor", "S", s_a1)
            for g in range(2):
                for k in range(4):
                    fc = 4 * g + k
                    ii = t.transpose(ptb[g][:, 128 * k:128 * k + 128],
                                     ffnT[:, fc * R + 128 * rt:fc * R + 128 * rt + 128],
                                     identS[:, :])
                inc(ii, "T")
        sec("tensor", _)
        t_ffn = cnt["T"]

        def _(v, rt=rt, par=par, xsn=xsn, t_ffn=t_ffn, s_a1=s_a1):
            wt(v, "vector", "X", xsn)
            wt(v, "vector", "T", t_ffn)
            wt(v, "vector", "S", s_a1)
            if rt >= 2:
                wt(v, "vector", "O")
            for g in range(2):
                ii = v.tensor_tensor(outb2[par][:, 512 * g:512 * g + 512],
                                     xslt[par][:, 512 * g:512 * g + 512],
                                     ptb[g][:, 0:512], ALU.add)
            sv = inc(ii, "V")
            v.wait_ge(sems["V"], sv)
            ii = v.tensor_add(outb2[par][:, :], outb2[par][:, :], xslb[par][:, :])
            inc(ii, "V")
        sec("vector", _)
        vsn = cnt["V"]

        def _(sy, rt=rt, par=par, vsn=vsn):
            wt(sy, "sync", "V", vsn)
            ii = sy.dma_start(d_out.rearrange("b j f -> j b f")[bass.ts(rt, 32), :, :],
                              outb2[par][:, :])
            inc(ii, "O")
        sec("sync", _)

    def _(sy):
        wt(sy, "sync", "O", cnt["O"])
    sec("sync", _)

    root.close()
    return nc


_NC_CACHE = {}


def kernel(**inputs):
    xf = np.ascontiguousarray(inputs["L0_attn_x"], dtype=np.float32)
    wat = np.ascontiguousarray(inputs["L0_attn_w_attn"], dtype=np.float32)
    wpr = np.ascontiguousarray(inputs["L0_attn_w_proj"], dtype=np.float32)
    winp = inputs["L0_ffn_in_proj_w"]
    inb = inputs["L0_ffn_in_proj_b"]
    woutw = inputs["L0_ffn_out_w"]
    outb = inputs["L0_ffn_out_b"]
    g1 = inputs["L0_ffn_ln1_g"]
    bb1 = inputs["L0_ffn_ln1_b"]
    rw = inputs["L0_ffn_router_w"]
    rb = inputs["L0_ffn_router_b"]
    w1 = inputs["L0_ffn_w1"]
    b1 = inputs["L0_ffn_b1"]
    w2 = inputs["L0_ffn_w2"]
    b2 = inputs["L0_ffn_b2"]
    g2 = inputs["L0_ffn_ln2_g"]
    bb2 = inputs["L0_ffn_ln2_b"]

    slopes = _alibi_slopes(16)
    jj = np.arange(T, dtype=np.float32)
    qaug = np.zeros((6, T), dtype=np.float32)
    qaug[0] = 1.0
    qaug[1] = 1.0
    hi_i, lo_i = _hi_lo(jj)
    qaug[2] = hi_i
    qaug[3] = lo_i
    qaug[4] = hi_i
    qaug[5] = lo_i

    def kaug_for(s):
        ka = np.zeros((8, 6, T), dtype=np.float32)
        for hl in range(8):
            sl = slopes[8 * s + hl]
            hsj, lsj = _hi_lo(sl * jj)
            hs, ls_ = _hi_lo(np.array([sl], dtype=np.float32))
            ka[hl, 0] = hsj
            ka[hl, 1] = lsj
            ka[hl, 2] = -hs[0]
            ka[hl, 3] = -hs[0]
            ka[hl, 4] = -ls_[0]
            ka[hl, 5] = -ls_[0]
        return ka

    def wqkv_for(s):
        out = np.empty((4, C, 384), dtype=np.float32)
        for p in range(4):
            r0 = 512 * s + 128 * p
            out[p, :, 0:128] = wat[r0:r0 + 128, :].T
            out[p, :, 128:256] = wat[1024 + r0:1024 + r0 + 128, :].T
            out[p, :, 256:384] = wat[2048 + r0:2048 + r0 + 128, :].T
        return out

    def occhunks(wT):
        return np.ascontiguousarray(wT.reshape(C, NFC, 128).transpose(1, 0, 2))

    wprojT = occhunks(np.ascontiguousarray(wpr.T))
    winT = np.ascontiguousarray(
        np.ascontiguousarray(winp.T).reshape(C, 24, 128).transpose(1, 0, 2))
    woutT = occhunks(np.ascontiguousarray(woutw.T))
    w1T = np.ascontiguousarray(
        w1.transpose(0, 2, 1).reshape(E, C, NFC, 128).transpose(0, 2, 1, 3))
    w2T = np.ascontiguousarray(
        w2.transpose(0, 2, 1).reshape(E, C, NFC, 128).transpose(0, 2, 1, 3))

    shared = {
        "wproj": wprojT, "win": winT,
        "inb": np.ascontiguousarray(inb.reshape(24, 128), dtype=np.float32),
        "wout": woutT,
        "outb": np.ascontiguousarray(outb.reshape(NFC, 128), dtype=np.float32),
        "g1": np.ascontiguousarray(g1.reshape(NFC, 128), dtype=np.float32),
        "bb1": np.ascontiguousarray(bb1.reshape(NFC, 128), dtype=np.float32),
        "rw": np.ascontiguousarray(rw.T, dtype=np.float32),
        "rb": np.ascontiguousarray(rb.reshape(1, 8), dtype=np.float32),
        "w1": w1T, "b1": np.ascontiguousarray(b1, dtype=np.float32),
        "w2": w2T, "b2": np.ascontiguousarray(b2, dtype=np.float32),
        "g2": np.ascontiguousarray(g2.reshape(NFC, 128), dtype=np.float32),
        "bb2": np.ascontiguousarray(bb2.reshape(NFC, 128), dtype=np.float32),
        "qaug": qaug,
    }
    wq_s = [wqkv_for(0), wqkv_for(1)]
    ka_s = [kaug_for(0), kaug_for(1)]

    key = "nc"
    if key not in _NC_CACHE:
        _NC_CACHE[key] = build_nc()
    nc = _NC_CACHE[key]

    in_maps = []
    for c in range(NCN):
        b, s = c // 2, c % 2
        m = dict(shared)
        m["xb"] = np.ascontiguousarray(xf[b])
        m["xsl"] = np.ascontiguousarray(xf[:, JS * c:JS * (c + 1), :])
        m["wqkv"] = wq_s[s]
        m["kaug"] = ka_s[s]
        in_maps.append(m)

    res = run_bass_kernel_spmd(nc, in_maps, core_ids=list(range(NCN)))
    out = np.concatenate([res.results[c]["out"] for c in range(NCN)], axis=1)
    return out.astype(np.float32)
